# version 21
# baseline (speedup 1.0000x reference)
import math
from collections import defaultdict

import numpy as np
import ml_dtypes

import concourse.bass as bass
import concourse.mybir as mybir
from concourse.bass_utils import run_bass_kernel_spmd

F32 = mybir.dt.float32
BF16 = mybir.dt.bfloat16
AF = mybir.ActivationFunctionType
ALU = mybir.AluOpType
AX = mybir.AxisListType

N = 1024
T = 4096
NT = 32
NB = 16
E = 8
PI = float(np.pi)

_cache = {}


def _consts():
    if "c" in _cache:
        return _cache["c"]
    n = np.arange(N, dtype=np.int64)
    ang = 2.0 * np.pi * ((n[:, None] * n[None, :]) % N).astype(np.float64) / N
    tabc = np.cos(ang).astype(ml_dtypes.bfloat16)
    tabs = np.sin(ang).astype(ml_dtypes.bfloat16)
    ka = (np.abs(np.fft.fftfreq(N)).astype(np.float32) + np.float32(1e-8))
    KA = ka.reshape(8, 128).T.copy().astype(np.float32)
    KA2 = (KA.astype(np.float64) ** 2).astype(np.float32)
    MISC = np.zeros((128, 8), np.float32)
    MISC[:, 0] = np.pi / 2
    MISC[:, 1] = 1e-5
    ONESF = np.ones((1, 128), np.float32)
    ONESB = np.ones((1, 128), ml_dtypes.bfloat16)
    ONW = np.ones((12, 1), ml_dtypes.bfloat16)
    IDN8 = np.eye(8, dtype=ml_dtypes.bfloat16)
    _A = np.array([[1,1,1,0,1,1,1,0,0,0,1,1],[1,1,0,1,1,1,0,0,0,1,1,0],[1,0,1,1,1,0,0,0,1,1,0,1],[0,1,1,1,0,0,0,1,1,0,1,1],[0,0,0,0,1,1,1,1,1,1,1,1],[1,1,1,1,0,0,0,0,0,0,0,0],[0,0,0,0,1,1,1,1,0,0,0,0],[1,1,0,0,1,1,0,0,1,1,0,0],[1,0,1,0,1,0,1,0,1,0,1,0],[0,1,0,1,0,1,0,1,0,1,0,1],[1,0,0,1,1,0,0,1,1,0,0,1],[0,1,1,0,0,1,1,0,0,1,1,0]], dtype=np.float32)
    H = np.concatenate([_A.T, np.eye(12, dtype=np.float32)], axis=1)
    HMAT = H.T.astype(ml_dtypes.bfloat16)
    lx = np.log(np.array([2., 4., 8., 16., 32.], np.float64))
    den = 5.0 * np.sum(lx * lx) - np.sum(lx) ** 2
    gam = (np.sum(lx) - 5.0 * lx) / den
    GAM = np.zeros((1, 8), np.float32)
    GAM[0, :5] = gam.astype(np.float32)
    _cache["c"] = dict(TABC=tabc, TABS=tabs, KA=KA, KA2=KA2, MISC=MISC, ONESF=ONESF,
                       ONESB=ONESB, ONW=ONW, IDN8=IDN8, HMAT=HMAT, GAM=GAM)
    return _cache["c"]


def build_nc():
    nc = bass.Bass("TRN2", target_bir_lowering=False, dynamic_dma_scratch_size=16384)

    x_d = nc.dram_tensor("x", [T, N], F32, kind="ExternalInput")
    xf_d = nc.dram_tensor("xf", [1, N], F32, kind="ExternalInput")
    wel_d = nc.dram_tensor("wel", [N, 24], F32, kind="ExternalInput")
    bel_d = nc.dram_tensor("bel", [1, 24], F32, kind="ExternalInput")
    wle_d = nc.dram_tensor("wle", [24, N], F32, kind="ExternalInput")
    ble_d = nc.dram_tensor("ble", [1, N], F32, kind="ExternalInput")
    rw_d = nc.dram_tensor("rw", [E, N], F32, kind="ExternalInput")
    ew_d = nc.dram_tensor("ew", [E, N, N], F32, kind="ExternalInput")
    tabc_d = nc.dram_tensor("tabc", [N, N], BF16, kind="ExternalInput")
    tabs_d = nc.dram_tensor("tabs", [N, N], BF16, kind="ExternalInput")
    ka_d = nc.dram_tensor("ka", [128, 8], F32, kind="ExternalInput")
    ka2_d = nc.dram_tensor("ka2", [128, 8], F32, kind="ExternalInput")
    misc_d = nc.dram_tensor("misc", [128, 8], F32, kind="ExternalInput")
    onesf_d = nc.dram_tensor("onesf", [1, 128], F32, kind="ExternalInput")
    onesb_d = nc.dram_tensor("onesb", [1, 128], BF16, kind="ExternalInput")
    onw_d = nc.dram_tensor("onw", [12, 1], BF16, kind="ExternalInput")
    idn8_d = nc.dram_tensor("idn8", [8, 8], BF16, kind="ExternalInput")
    hmat_d = nc.dram_tensor("hmat", [24, 12], BF16, kind="ExternalInput")
    gam_d = nc.dram_tensor("gam", [1, 8], F32, kind="ExternalInput")

    y_d = nc.dram_tensor("y", [T, N], F32, kind="ExternalOutput")
    xnscr = nc.dram_tensor("xnscr", [T, N], BF16)
    hhbig = nc.dram_tensor("hhbig", [1, 128 * 2177 + 2048], BF16)

    from contextlib import ExitStack
    with ExitStack() as ctx:
        sb = lambda name, shape, dt: ctx.enter_context(nc.sbuf_tensor(name, shape, dt))

        EW = sb("EWsb", [128, 64 * 1024], BF16)
        C = sb("Csb", [128, 8 * 1024], BF16)
        CW = sb("CWsb", [128, 8 * 24], BF16)
        WELB = sb("WELB", [128, 8 * 24], BF16)
        WLE = sb("WLEsb", [24, N], BF16)
        BLEB = sb("BLEB", [1, N], BF16)
        RWB = sb("RWB", [8, N], BF16)
        RWT = sb("RWT", [128, 64], BF16)
        BELC = sb("BELC", [24, 1], F32)
        MISC = sb("MISCsb", [128, 8], F32)
        ONESF = sb("ONESFsb", [1, 128], F32)
        ONESB = sb("ONESBsb", [1, 128], BF16)
        ONW = sb("ONWsb", [12, 1], BF16)
        IDN8 = sb("IDN8sb", [8, 8], BF16)
        HMAT = sb("HMATsb", [24, 12], BF16)
        XB = [sb(f"XB{i}", [128, 2 * N], F32) for i in range(2)]

        sctx = ExitStack()
        ssb = lambda name, shape, dt: sctx.enter_context(nc.sbuf_tensor(name, shape, dt))
        psT = sctx.enter_context(nc.psum_tensor("psT", [128, 64], BF16))
        TABC = ssb("TABC", [128, 8 * 1024], BF16)
        TABS = ssb("TABS", [128, 8 * 1024], BF16)
        GAM = ssb("GAMsb", [1, 8], F32)
        HH = ssb("HHsb", [1, 2 * N], BF16)
        MEANS = ssb("MEANS", [1, 8], F32)
        CNTS = ssb("CNTS", [1, 8], F32)
        LYS = ssb("LYS", [1, 8], F32)
        DV = ssb("DVsb", [1, 1], F32)
        DCOL = ssb("DCOL", [128, 1], F32)
        FCOLS = ssb("FCOLS", [128, 8], F32)
        KA = ssb("KAsb", [128, 8], F32)
        KA2 = ssb("KA2sb", [128, 8], F32)
        FW1 = ssb("FW1", [128, 8], F32)
        FW2 = ssb("FW2", [128, 8], F32)
        FW3 = ssb("FW3", [128, 8], F32)
        FW4 = ssb("FW4", [128, 8], F32)
        FRC = ssb("FRC", [128, 8], BF16)
        NFIC = ssb("NFIC", [128, 8], BF16)
        A1B = ssb("A1B", [128, 8 * 24], BF16)
        A2B = ssb("A2B", [128, 8 * 24], BF16)
        UV1 = ssb("UV1", [128, 24], F32)
        UV2 = ssb("UV2", [128, 24], F32)
        sctx.close()

        psA = ctx.enter_context(nc.psum_tensor("psA", [128, N], F32))
        psB = ctx.enter_context(nc.psum_tensor("psB", [128, N], F32))
        psS = ctx.enter_context(nc.psum_tensor("psS", [128, 512], F32))
        psLg = ctx.enter_context(nc.psum_tensor("psLg", [128, 16], F32))
        psC = ctx.enter_context(nc.psum_tensor("psC", [128, N], F32))

        XN = sb("XNsb", [128, 2 * N], BF16)
        XNT = [sb(f"XNT{i}", [128, 8 * 256], BF16) for i in range(2)]
        X1 = sb("X1sb", [128, 2 * N], BF16)
        S = sb("Ssb", [128, 2 * N], BF16)
        ACC = [sb(f"ACC{i}", [128, N], F32) for i in range(2)]
        G = [sb(f"G{i}", [128, N], BF16) for i in range(2)]
        SUMC = sb("SUMC", [128, 2], F32)
        SQC = sb("SQC", [128, 2], F32)
        MC = sb("MCsb", [128, 2], F32)
        VC = sb("VCsb", [128, 2], F32)
        T1C = sb("T1C", [128, 2], F32)
        T2C = sb("T2C", [128, 2], F32)
        LNT = sb("LNT", [128, 2], F32)
        RSTD = sb("RSTD", [128, 2], F32)
        NM2 = sb("NM2", [128, 2], F32)
        NMX = sb("NMX", [128, 2], F32)
        U = [sb(f"U{i}", [128, 8], F32) for i in range(2)]
        ZC = sb("ZC", [128, 2], F32)
        ZI = sb("ZI", [128, 2], F32)
        PR = [sb(f"PR{i}", [128, 8], F32) for i in range(2)]
        V1 = sb("V1", [128, 2], F32)
        V2 = sb("V2", [128, 2], F32)
        MK = sb("MKsb", [128, 16], F32)
        MS = sb("MSsb", [128, 16], F32)
        DN = sb("DNsb", [128, 2], F32)
        DI = sb("DIsb", [128, 2], F32)
        SL = sb("SLsb", [128, 16], F32)
        MT = sb("MTsb", [128, 16], F32)
        Q1 = sb("Q1sb", [128, 2], F32)
        Q2 = sb("Q2sb", [128, 2], F32)
        MCOL = sb("MCOL", [128, 16], F32)
        CCOL = sb("CCOL", [128, 4], F32)
        BITS = sb("BITS", [24, 256], BF16)
        SYN = sb("SYNsb", [12, 256], BF16)
        FIXT1 = sb("FIXT1", [1, 256], F32)
        FIXT2 = sb("FIXT2", [1, 256], F32)
        FIXB = sb("FIXB", [1, 256], BF16)
        FSYN = sb("FSYN", [24, 256], BF16)
        TB1 = sb("TB1", [24, 256], BF16)
        TB2 = sb("TB2", [24, 256], BF16)
        CORR = sb("CORR", [24, 256], BF16)

        sems = {}
        for s in ("dX", "dXN", "dXNT", "dOUT", "dW", "dC", "dM", "dH",
                  "SV", "SS", "ST"):
            sems[s] = ctx.enter_context(nc.semaphore(s))

        block = ctx.enter_context(nc.Block())

        plan = {k: [] for k in ("sync", "scalar", "vector", "tensor", "gpsimd")}
        ctr = defaultdict(int)
        snap = {}

        def W(eng, sem, val):
            if val > 0:
                plan[eng].append(("w", sem, val))

        def OP(eng, fn, incs=()):
            for s, k in incs:
                ctr[s] += k
            plan[eng].append(("o", fn, [list(i) for i in incs]))

        def now(s):
            return ctr[s]

        def inc_prev(eng, sem, k=1):
            for item in reversed(plan[eng]):
                if item[0] == "o":
                    ctr[sem] += k
                    item[2].append([sem, k])
                    return
            raise RuntimeError("no prior op")

        def selfsem(eng, sem):
            inc_prev(eng, sem)
            W(eng, sem, now(sem))

        def dma(eng, dst, src, sem, transpose=False):
            OP(eng, lambda e, d=dst, s=src, t=transpose: e.dma_start(d, s, transpose=t),
               [(sem, 16)])

        AP = bass.AP
        dma("sync", KA[:, :], ka_d.ap(), "dM")
        dma("sync", KA2[:, :], ka2_d.ap(), "dM")
        dma("sync", MISC[:, :], misc_d.ap(), "dM")
        dma("sync", ONESF[:, :], onesf_d.ap(), "dM")
        dma("sync", ONESB[:, :], onesb_d.ap(), "dM")
        dma("sync", ONW[:, :], onw_d.ap(), "dM")
        dma("sync", IDN8[:, :], idn8_d.ap(), "dM")
        dma("sync", HMAT[:, :], hmat_d.ap(), "dM")
        dma("sync", GAM[:, :], gam_d.ap(), "dM")
        dma("sync", XB[1][0:1, 0:N], xf_d.ap(), "dM")
        dma("sync", BELC[:, :], AP(bel_d, 0, [[1, 24], [1, 1]]), "dM")
        snap["dM_small"] = now("dM")
        for q in range(8):
            dma("sync", TABC[:, 1024 * q:1024 * (q + 1)],
                AP(tabc_d, q * 131072, [[1024, 128], [1, 1024]]), "dM")
            dma("sync", TABS[:, 1024 * q:1024 * (q + 1)],
                AP(tabs_d, q * 131072, [[1024, 128], [1, 1024]]), "dM")
        snap["dM_tabs"] = now("dM")

        dma("gpsimd", WLE[:, :], wle_d.ap(), "dW")
        dma("gpsimd", BLEB[:, :], ble_d.ap(), "dW")
        dma("gpsimd", RWB[:, :], rw_d.ap(), "dW")
        for q in range(8):
            dma("gpsimd", WELB[:, 24 * q:24 * (q + 1)],
                AP(wel_d, q * 3072, [[24, 128], [1, 24]]), "dW")
        snap["dW_small"] = now("dW")
        for e in range(E):
            for h in range(2):
                dma("gpsimd", EW[:, 8192 * e + 4096 * h:8192 * e + 4096 * (h + 1)].rearrange(
                    "p (k c) -> p k c", c=1024),
                    AP(ew_d, e * N * N + h * 4 * 131072, [[1024, 128], [131072, 4], [1, 1024]]),
                    "dW")
            snap[f"dW_e{e}"] = now("dW")

        def xload(b):
            if b >= 2:
                W("sync", "SV", snap[("final", b - 2, 1)])
                W("sync", "dOUT", 16 * (2 * (b - 2) + 2))
            dst = XB[b % 2][:, :].rearrange("p (t c) -> p t c", c=N)
            src = AP(x_d, 256 * b * N, [[N, 128], [128 * N, 2], [1, N]])
            dma("sync", dst, src, "dX")
            snap[("dX", b)] = now("dX")

        xload(0)

        W("vector", "dM", snap["dM_small"])
        for i, m in enumerate([4, 16, 64, 256, 1024]):
            OP("vector", lambda e, i=i, m=m: e.reduce_sum(MEANS[0:1, i:i + 1], XB[1][0:1, 0:m], AX.X))
        selfsem("vector", "SV")
        offs = [0, 4, 20, 84, 340]
        for i, m in enumerate([4, 16, 64, 256, 1024]):
            OP("vector", lambda e, i=i, m=m, o=offs[i]: e.tensor_scalar(
                HH[0:1, o:o + m], XB[1][0:1, 0:m], float(m), MEANS[0:1, i:i + 1], ALU.mult, ALU.is_gt))
        selfsem("vector", "SV")
        for i, m in enumerate([4, 16, 64, 256, 1024]):
            OP("vector", lambda e, i=i, m=m, o=offs[i]: e.reduce_sum(CNTS[0:1, i:i + 1], HH[0:1, o:o + m], AX.X))
        selfsem("vector", "SV")
        OP("vector", lambda e: e.tensor_scalar(CNTS[0:1, 0:5], CNTS[0:1, 0:5], 1.0, None, ALU.max),
           [("SV", 1)])
        snap["cnts"] = now("SV")
        W("sync", "SV", snap["cnts"])
        xload(1)

        W("scalar", "SV", snap["cnts"])
        OP("scalar", lambda e: e.activation(LYS[0:1, 0:5], CNTS[0:1, 0:5], AF.Ln), [("SS", 1)])
        snap["lys"] = now("SS")

        W("vector", "SS", snap["lys"])
        OP("vector", lambda e: e.tensor_tensor(LYS[0:1, 0:5], LYS[0:1, 0:5], GAM[0:1, 0:5], ALU.mult))
        selfsem("vector", "SV")
        OP("vector", lambda e: e.reduce_sum(DV[0:1, 0:1], LYS[0:1, 0:5], AX.X))
        selfsem("vector", "SV")
        OP("vector", lambda e: e.tensor_scalar(DV[0:1, 0:1], DV[0:1, 0:1], 1.0, 2.0, ALU.max, ALU.min),
           [("SV", 1)])
        snap["dval"] = now("SV")

        W("tensor", "SV", snap["dval"])
        W("tensor", "dM", snap["dM_small"])
        OP("tensor", lambda e: e.matmul(psC[:, 0:1], ONESF[0:1, :], DV[0:1, 0:1],
                                        start=True, stop=True), [("ST", 1)])
        snap["dbc"] = now("ST")

        W("vector", "ST", snap["dbc"])
        OP("vector", lambda e: e.tensor_copy(DCOL[:, :], psC[:, 0:1]))
        selfsem("vector", "SV")
        OP("vector", lambda e: e.tensor_scalar(FCOLS[:, 0:1], DCOL[:, :], 0.2, 1.0, ALU.mult, ALU.add))
        OP("vector", lambda e: e.tensor_scalar(FCOLS[:, 2:3], DCOL[:, :], 0.5, 1.0, ALU.mult, ALU.add))
        OP("vector", lambda e: e.tensor_scalar(FCOLS[:, 3:4], DCOL[:, :], 0.1, None, ALU.mult))
        selfsem("vector", "SV")
        OP("vector", lambda e: e.tensor_tensor(FCOLS[:, 1:2], FCOLS[:, 0:1], DCOL[:, :], ALU.mult))
        selfsem("vector", "SV")
        OP("vector", lambda e: e.tensor_scalar(FW1[:, :], KA[:, :], FCOLS[:, 2:3], FCOLS[:, 1:2],
                                               ALU.mult, ALU.add))
        OP("vector", lambda e: e.tensor_scalar(FW2[:, :], KA2[:, :], FCOLS[:, 3:4], None, ALU.mult))
        selfsem("vector", "SV")
        OP("vector", lambda e: e.tensor_scalar(FW3[:, :], FW1[:, :], PI, None, ALU.is_gt))
        OP("vector", lambda e: e.scalar_tensor_tensor(FW2[:, :], KA[:, :], -1.0, FW2[:, :],
                                                      ALU.mult, ALU.add))
        selfsem("vector", "SV")
        OP("vector", lambda e: e.scalar_tensor_tensor(FW1[:, :], FW3[:, :], -2.0 * PI, FW1[:, :],
                                                      ALU.mult, ALU.add))
        OP("vector", lambda e: e.tensor_scalar(FW2[:, :], FW2[:, :], FCOLS[:, 1:2], None, ALU.add),
           [("SV", 1)])
        snap["args"] = now("SV")

        W("scalar", "SV", snap["args"])
        OP("scalar", lambda e: e.activation(FW1[:, :], FW1[:, :], AF.Sin))
        OP("scalar", lambda e: e.activation(FW3[:, :], FW2[:, :], AF.Sin, bias=MISC[:, 0:1], scale=-1.0))
        selfsem("scalar", "SS")
        OP("scalar", lambda e: e.activation(FW2[:, :], FW2[:, :], AF.Sin), [("SS", 1)])
        snap["sins"] = now("SS")

        W("vector", "SS", snap["sins"])
        OP("vector", lambda e: e.tensor_tensor(FW3[:, :], FW1[:, :], FW3[:, :], ALU.mult))
        OP("vector", lambda e: e.tensor_tensor(FW2[:, :], FW1[:, :], FW2[:, :], ALU.mult))
        selfsem("vector", "SV")
        OP("vector", lambda e: e.tensor_scalar(FW3[:, :], FW3[:, :], 1.0 / N, None, ALU.mult))
        OP("vector", lambda e: e.tensor_scalar(FW2[:, :], FW2[:, :], -1.0 / N, None, ALU.mult))
        selfsem("vector", "SV")
        OP("vector", lambda e: e.tensor_copy(FRC[:, :], FW3[:, :]))
        OP("vector", lambda e: e.tensor_scalar(FW4[:, :], FW2[:, :], -1.0, None, ALU.mult))
        OP("vector", lambda e: e.tensor_copy(NFIC[:, :], FW2[:, :]), [("SV", 1)])
        snap["frc"] = now("SV")

        W("tensor", "dW", snap["dW_small"])
        for r in range(8):
            OP("tensor", lambda e, r=r: e.transpose(psT[:, 8 * r:8 * (r + 1)],
                                                    RWB[0:8, 128 * r:128 * (r + 1)],
                                                    IDN8[:, :]), [("ST", 1)] if r == 7 else ())
        W("vector", "ST", now("ST"))
        OP("vector", lambda e: e.tensor_copy(RWT[:, :], psT[:, :]), [("SV", 1)])
        W("tensor", "SV", now("SV"))
        snap["rwt"] = now("SV")

        W("tensor", "dM", snap["dM_tabs"])
        W("tensor", "dW", snap["dW_small"])
        for r in range(8):
            for q in range(8):
                OP("tensor", lambda e, r=r, q=q: e.matmul(
                    psB[:, 24 * r:24 * (r + 1)],
                    TABC[:, 1024 * q + 128 * r:1024 * q + 128 * r + 128],
                    WELB[:, 24 * q:24 * (q + 1)], start=(q == 0), stop=(q == 7)))
            for q in range(8):
                OP("tensor", lambda e, r=r, q=q: e.matmul(
                    psB[:, 192 + 24 * r:192 + 24 * (r + 1)],
                    TABS[:, 1024 * q + 128 * r:1024 * q + 128 * r + 128],
                    WELB[:, 24 * q:24 * (q + 1)], start=(q == 0), stop=(q == 7)))
        inc_prev("tensor", "ST")
        W("vector", "ST", now("ST"))
        W("vector", "SV", snap["frc"])
        for r in range(8):
            OP("vector", lambda e, r=r: e.tensor_scalar(A1B[:, 24 * r:24 * (r + 1)],
                                                        psB[:, 24 * r:24 * (r + 1)],
                                                        FW3[:, r:r + 1], None, ALU.mult))
            OP("vector", lambda e, r=r: e.tensor_scalar(A2B[:, 24 * r:24 * (r + 1)],
                                                        psB[:, 192 + 24 * r:192 + 24 * (r + 1)],
                                                        FW3[:, r:r + 1], None, ALU.mult))
        selfsem("vector", "SV")
        for r in range(8):
            OP("vector", lambda e, r=r: e.scalar_tensor_tensor(
                A1B[:, 24 * r:24 * (r + 1)], psB[:, 192 + 24 * r:192 + 24 * (r + 1)],
                FW2[:, r:r + 1], A1B[:, 24 * r:24 * (r + 1)], ALU.mult, ALU.add))
            OP("vector", lambda e, r=r: e.scalar_tensor_tensor(
                A2B[:, 24 * r:24 * (r + 1)], psB[:, 24 * r:24 * (r + 1)],
                FW4[:, r:r + 1], A2B[:, 24 * r:24 * (r + 1)], ALU.mult, ALU.add))
        inc_prev("vector", "SV")
        W("tensor", "SV", now("SV"))
        for r in range(8):
            for q in range(8):
                OP("tensor", lambda e, r=r, q=q: e.matmul(
                    psC[:, 24 * r:24 * (r + 1)],
                    TABC[:, 1024 * q + 128 * r:1024 * q + 128 * r + 128],
                    A1B[:, 24 * q:24 * (q + 1)], start=(q == 0), stop=False))
            for q in range(8):
                OP("tensor", lambda e, r=r, q=q: e.matmul(
                    psC[:, 24 * r:24 * (r + 1)],
                    TABS[:, 1024 * q + 128 * r:1024 * q + 128 * r + 128],
                    A2B[:, 24 * q:24 * (q + 1)], start=False, stop=(q == 7)))
        inc_prev("tensor", "ST")
        W("vector", "ST", now("ST"))
        OP("vector", lambda e: e.tensor_copy(CW[:, :], psC[:, 0:192]), [("SV", 1)])
        W("tensor", "SV", snap["frc"])
        W("tensor", "dM", snap["dM_tabs"])
        for hh_ in range(2):
            for r in range(8):
                OP("tensor", lambda e, hh_=hh_, r=r: e.matmul(
                    psA[0:1, 512 * hh_:512 * (hh_ + 1)], FRC[:, r:r + 1],
                    TABC[:, 1024 * r + 512 * hh_:1024 * r + 512 * (hh_ + 1)],
                    start=(r == 0), stop=False))
            for r in range(8):
                OP("tensor", lambda e, hh_=hh_, r=r: e.matmul(
                    psA[0:1, 512 * hh_:512 * (hh_ + 1)], NFIC[:, r:r + 1],
                    TABS[:, 1024 * r + 512 * hh_:1024 * r + 512 * (hh_ + 1)],
                    start=False, stop=(r == 7)))
        inc_prev("tensor", "ST")
        snap["hmm"] = now("ST")

        W("scalar", "ST", snap["hmm"])
        OP("scalar", lambda e: e.activation(HH[0:1, 0:N], psA[0:1, 0:N], AF.Identity))
        OP("scalar", lambda e: e.activation(HH[0:1, N:2 * N], psA[0:1, 0:N], AF.Identity),
           [("SS", 1)])
        snap["hhs"] = now("SS")

        W("sync", "SS", snap["hhs"])
        dma("sync", AP(hhbig, 0, [[1, 1], [1, 2 * N]]), HH[:, :], "dH")
        for k in range(7):
            W("sync", "dH", now("dH"))
            p = 2 ** k
            dma("sync", AP(hhbig, p * 2178, [[2177, p], [1, 2 * N]]),
                AP(hhbig, 0, [[2177, p], [1, 2 * N]]), "dH")
        W("sync", "dH", now("dH"))
        for r in range(8):
            dma("sync", C[:, 1024 * r:1024 * (r + 1)],
                AP(hhbig, N - 128 * r, [[2177, 128], [1, N]]), "dC")
        snap["dC_all"] = now("dC")

        snap["cw"] = now("SV")
        snap["cw_t"] = now("ST")
        W("vector", "ST", snap["cw_t"])
        OP("vector", lambda e: e.memset(FSYN[:, :], 0.0))

        psEO = [psA, psB]

        def lhs_slice(b, t, r):
            return XNT[b % 2][:, 256 * r + 128 * t:256 * r + 128 * t + 128]

        def prologue_a(b):
            W("scalar", "dX", snap[("dX", b)])
            if b == 0:
                W("scalar", "SV", snap["cw"])
                W("scalar", "ST", snap["cw_t"])
                W("scalar", "dH", 16)
            if b >= 1:
                W("scalar", "dXN", 16 * b)
            for t in range(2):
                OP("scalar", lambda e, b=b, t=t: e.activation(
                    XN[:, N * t:N * (t + 1)], XB[b % 2][:, N * t:N * (t + 1)], AF.Square,
                    accum_out=SQC[:, t:t + 1]))
                OP("scalar", lambda e, b=b, t=t: e.activation(
                    XN[:, N * t:N * (t + 1)], XB[b % 2][:, N * t:N * (t + 1)], AF.Identity,
                    accum_out=SUMC[:, t:t + 1]))
            inc_prev("scalar", "SS")
            snap[("sq", b)] = now("SS")

        def prologue_b(b):
            if b == 0:
                W("vector", "SV", snap["cw"])
                W("vector", "ST", snap["cw_t"])
                W("vector", "dH", 16)
            W("vector", "SS", snap[("sq", b)])
            for t in range(2):
                OP("vector", lambda e, t=t: e.tensor_scalar(MC[:, t:t + 1], SUMC[:, t:t + 1],
                                                            1.0 / N, None, ALU.mult))
                OP("vector", lambda e, t=t: e.tensor_scalar(T1C[:, t:t + 1], SQC[:, t:t + 1],
                                                            1.0 / N, None, ALU.mult))
            selfsem("vector", "SV")
            for t in range(2):
                OP("vector", lambda e, t=t: e.tensor_tensor(T2C[:, t:t + 1], MC[:, t:t + 1],
                                                            MC[:, t:t + 1], ALU.mult))
            selfsem("vector", "SV")
            for t in range(2):
                OP("vector", lambda e, t=t: e.tensor_tensor(VC[:, t:t + 1], T1C[:, t:t + 1],
                                                            T2C[:, t:t + 1], ALU.subtract))
            inc_prev("vector", "SV")
            snap[("vc", b)] = now("SV")

            W("scalar", "SV", snap[("vc", b)])
            for t in range(2):
                OP("scalar", lambda e, t=t: e.activation(LNT[:, t:t + 1], VC[:, t:t + 1], AF.Ln,
                                                         bias=MISC[:, 1:2]))
            selfsem("scalar", "SS")
            for t in range(2):
                OP("scalar", lambda e, t=t: e.activation(RSTD[:, t:t + 1], LNT[:, t:t + 1], AF.Exp,
                                                         scale=-0.5))
            inc_prev("scalar", "SS")
            snap[("rstd", b)] = now("SS")

            W("vector", "SS", snap[("rstd", b)])
            for t in range(2):
                OP("vector", lambda e, t=t: e.tensor_tensor(NM2[:, t:t + 1], RSTD[:, t:t + 1],
                                                            MC[:, t:t + 1], ALU.mult))
            selfsem("vector", "SV")
            for t in range(2):
                OP("vector", lambda e, t=t: e.tensor_scalar(NM2[:, t:t + 1], NM2[:, t:t + 1],
                                                            -1.0, None, ALU.mult))
            inc_prev("vector", "SV")
            snap[("nm", b)] = now("SV")

        def prologue_c(b):
            W("scalar", "SV", snap[("nm", b)])
            for t in range(2):
                OP("scalar", lambda e, b=b, t=t: e.activation(
                    XN[:, N * t:N * (t + 1)], XB[b % 2][:, N * t:N * (t + 1)], AF.Identity,
                    scale=RSTD[:, t:t + 1], bias=NM2[:, t:t + 1]))
            inc_prev("scalar", "SS")
            W("scalar", "SS", now("SS"))
            OP("scalar", lambda e, b=b: e.dma_start(
                AP(xnscr, 256 * b * N, [[N, 128], [128 * N, 2], [1, N]]),
                XN[:, :].rearrange("p (t c) -> p t c", c=N)), [("dXN", 16)])
            snap[("dXN", b)] = now("dXN")

            W("sync", "dXN", snap[("dXN", b)])
            if b >= 2:
                W("sync", "ST", snap[("tb_end", b - 2)])
            for r in range(8):
                dma("sync", XNT[b % 2][:, 256 * r:256 * (r + 1)],
                    AP(xnscr, (256 * b) * N + 128 * r, [[N, 256], [1, 128]]), "dXNT",
                    transpose=True)
            snap[("dXNT", b)] = now("dXNT")

        def emit_router(bb):
            if bb >= 1:
                W("tensor", "SS", snap[("exp", bb - 1, 1)])
            W("tensor", "dXNT", snap[("dXNT", bb)])
            for t in range(2):
                for r in range(8):
                    OP("tensor", lambda e, bb=bb, t=t, r=r: e.matmul(
                        psLg[:, 8 * t:8 * (t + 1)], lhs_slice(bb, t, r), RWT[:, 8 * r:8 * (r + 1)],
                        start=(r == 0), stop=(r == 7)))
            inc_prev("tensor", "ST")
            snap[("logits", bb)] = now("ST")
            W("vector", "ST", snap[("logits", bb)])
            for t in range(2):
                OP("vector", lambda e, t=t: e.reduce_max(NMX[:, t:t + 1],
                                                         psLg[:, 8 * t:8 * (t + 1)],
                                                         AX.X, negate=True))
            inc_prev("vector", "SV")
            snap[("nmx", bb)] = now("SV")
            W("scalar", "SV", snap[("nmx", bb)])
            for t in range(2):
                OP("scalar", lambda e, t=t: e.activation(U[t][:, :], psLg[:, 8 * t:8 * (t + 1)],
                                                         AF.Exp, bias=NMX[:, t:t + 1]))
            inc_prev("scalar", "SS")
            snap[("exp", bb, 1)] = now("SS")

        prologue_a(0)
        prologue_b(0)
        prologue_c(0)
        emit_router(0)

        for b in range(NB):
            W("tensor", "dXNT", snap[("dXNT", b)])
            if b >= 1:
                W("tensor", "SV", snap[("corr", b - 1)])
            for r in range(8):
                OP("tensor", lambda e, b=b, r=r: e.matmul(
                    psS[0:24, 0:256], CW[:, 24 * r:24 * (r + 1)],
                    XNT[b % 2][:, 256 * r:256 * (r + 1)], start=(r == 0), stop=(r == 7)))
            inc_prev("tensor", "ST")
            snap[("latT", b)] = now("ST")

            W("vector", "ST", snap[("latT", b)])
            OP("vector", lambda e: e.tensor_scalar(BITS[:, :], psS[0:24, 0:256], BELC[:, 0:1],
                                                   0.0, ALU.add, ALU.is_gt), [("SV", 1)])
            snap[("bits", b)] = now("SV")

            W("vector", "SS", snap[("exp", b, 1)])
            for t in range(2):
                OP("vector", lambda e, t=t: e.reduce_sum(ZC[:, t:t + 1], U[t][:, :], AX.X))
            selfsem("vector", "SV")
            for t in range(2):
                OP("vector", lambda e, t=t: e.reciprocal(ZI[:, t:t + 1], ZC[:, t:t + 1]))
            selfsem("vector", "SV")
            for t in range(2):
                OP("vector", lambda e, t=t: e.tensor_scalar(PR[t][:, :], U[t][:, :], ZI[:, t:t + 1],
                                                            None, ALU.mult))
            selfsem("vector", "SV")
            for t in range(2):
                OP("vector", lambda e, t=t: e.reduce_max(V1[:, t:t + 1], PR[t][:, :], AX.X))
            selfsem("vector", "SV")
            for t in range(2):
                OP("vector", lambda e, t=t: e.tensor_scalar(MK[:, 8 * t:8 * (t + 1)], PR[t][:, :],
                                                            V1[:, t:t + 1], None, ALU.is_ge))
            selfsem("vector", "SV")
            for t in range(2):
                OP("vector", lambda e, t=t: e.scalar_tensor_tensor(
                    MS[:, 8 * t:8 * (t + 1)], MK[:, 8 * t:8 * (t + 1)], -1e9, PR[t][:, :],
                    ALU.mult, ALU.add))
            selfsem("vector", "SV")
            for t in range(2):
                OP("vector", lambda e, t=t: e.reduce_max(V2[:, t:t + 1],
                                                         MS[:, 8 * t:8 * (t + 1)], AX.X))
            selfsem("vector", "SV")
            for t in range(2):
                OP("vector", lambda e, t=t: e.tensor_tensor(DN[:, t:t + 1], V1[:, t:t + 1],
                                                            V2[:, t:t + 1], ALU.add))
                OP("vector", lambda e, t=t: e.tensor_tensor(Q1[:, t:t + 1], V1[:, t:t + 1],
                                                            V1[:, t:t + 1], ALU.mult))
                OP("vector", lambda e, t=t: e.tensor_tensor(Q2[:, t:t + 1], V2[:, t:t + 1],
                                                            V2[:, t:t + 1], ALU.mult))
            selfsem("vector", "SV")
            for t in range(2):
                OP("vector", lambda e, t=t: e.tensor_scalar(DN[:, t:t + 1], DN[:, t:t + 1], 1e-6,
                                                            None, ALU.add))
                OP("vector", lambda e, t=t: e.tensor_tensor(Q1[:, t:t + 1], Q1[:, t:t + 1],
                                                            Q2[:, t:t + 1], ALU.add))
                OP("vector", lambda e, t=t: e.tensor_scalar(SL[:, 8 * t:8 * (t + 1)], PR[t][:, :],
                                                            V2[:, t:t + 1], None, ALU.is_ge))
            selfsem("vector", "SV")
            for t in range(2):
                OP("vector", lambda e, t=t: e.reciprocal(DI[:, t:t + 1], DN[:, t:t + 1]))
                OP("vector", lambda e, t=t: e.tensor_tensor(MT[:, 8 * t:8 * (t + 1)], PR[t][:, :],
                                                            SL[:, 8 * t:8 * (t + 1)], ALU.mult))
            selfsem("vector", "SV")
            for t in range(2):
                OP("vector", lambda e, t=t: e.tensor_scalar(MCOL[:, 8 * t:8 * (t + 1)],
                                                            MT[:, 8 * t:8 * (t + 1)],
                                                            DI[:, t:t + 1], None, ALU.mult))
                OP("vector", lambda e, b=b, t=t: e.tensor_scalar(
                    CCOL[:, 2 * (b % 2) + t:2 * (b % 2) + t + 1], Q1[:, t:t + 1], DI[:, t:t + 1],
                    None, ALU.mult))
            inc_prev("vector", "SV")
            snap[("m", b)] = now("SV")

            golay_t = iter([
                ("syn", lambda: (W("tensor", "SV", snap[("bits", b)]),
                                 OP("tensor", lambda e: e.matmul(psS[0:12, 256:512], HMAT[:, :],
                                                                 BITS[:, :], start=True, stop=True),
                                    [("ST", 1)]),
                                 snap.__setitem__(("synm", b), now("ST")))),
                ("w", lambda: (W("tensor", "SV", snap[("synv", b)]),
                               OP("tensor", lambda e: e.matmul(psS[0:1, 0:256], ONW[:, :],
                                                               SYN[:, :], start=True, stop=True),
                                  [("ST", 1)]),
                               snap.__setitem__(("wm", b), now("ST")))),
                ("fr", lambda: (W("tensor", "SV", snap[("fix", b)]),
                                OP("tensor", lambda e: e.matmul(psS[0:12, 0:256], ONESB[0:1, 0:12],
                                                                FIXB[:, :], start=True, stop=True),
                                   [("ST", 1)]),
                                snap.__setitem__(("frm", b), now("ST")))),
            ])
            golay_v = iter([
                ("synv", lambda: (W("vector", "ST", snap[("synm", b)]),
                                  OP("vector", lambda e: e.tensor_scalar(
                                      SYN[:, :], psS[0:12, 256:512], 0.5, None, ALU.is_gt),
                                     [("SV", 1)]),
                                  snap.__setitem__(("synv", b), now("SV")))),
                ("fixv", lambda: (W("vector", "ST", snap[("wm", b)]),
                                  OP("vector", lambda e: e.tensor_scalar(
                                      FIXT1[:, :], psS[0:1, 0:256], 0.5, None, ALU.is_gt)),
                                  OP("vector", lambda e: e.tensor_scalar(
                                      FIXT2[:, :], psS[0:1, 0:256], 3.5, None, ALU.is_lt)),
                                  selfsem("vector", "SV"),
                                  OP("vector", lambda e: e.tensor_tensor(
                                      FIXB[:, :], FIXT1[:, :], FIXT2[:, :], ALU.mult), [("SV", 1)]),
                                  snap.__setitem__(("fix", b), now("SV")))),
                ("corrv", lambda: (W("vector", "ST", snap[("frm", b)]),
                                   OP("vector", lambda e: e.tensor_tensor(
                                       FSYN[0:12, :], SYN[:, :], psS[0:12, 0:256], ALU.mult)),
                                   selfsem("vector", "SV"),
                                   OP("vector", lambda e: e.tensor_tensor(
                                       TB1[:, :], BITS[:, :], FSYN[:, :], ALU.mult)),
                                   OP("vector", lambda e: e.tensor_tensor(
                                       TB2[:, :], BITS[:, :], FSYN[:, :], ALU.add)),
                                   selfsem("vector", "SV"),
                                   OP("vector", lambda e: e.scalar_tensor_tensor(
                                       CORR[:, :], TB1[:, :], -2.0, TB2[:, :], ALU.mult, ALU.add),
                                      [("SV", 1)]),
                                   snap.__setitem__(("corr", b), now("SV")))),
            ])

            if b + 1 < NB:
                prologue_a(b + 1)

            for t in range(2):
                gt = 2 * b + t

                if b == 0 and t == 0:
                    W("tensor", "dC", snap["dC_all"])
                if t == 0:
                    if b >= 1:
                        W("tensor", "SV", snap[("s", b - 1, 1)])
                else:
                    W("tensor", "SV", snap[("s", b, 0)])
                for r in range(8):
                    for hf in range(2):
                        OP("tensor", lambda e, b=b, t=t, r=r, hf=hf: e.matmul(
                            psC[:, 512 * hf:512 * (hf + 1)], lhs_slice(b, t, r),
                            C[:, 1024 * r + 512 * hf:1024 * r + 512 * (hf + 1)],
                            start=(r == 0), stop=(r == 7)))
                inc_prev("tensor", "ST")
                snap[("x1", b, t)] = now("ST")

                W("vector", "ST", snap[("x1", b, t)])
                OP("vector", lambda e, t=t: e.tensor_copy(X1[:, N * t:N * (t + 1)], psC[:, :]),
                   [("SV", 1)])
                snap[("x1e", b, t)] = now("SV")

                for e_ in range(E):
                    ps = psEO[e_ % 2]
                    prev = (b, t, e_ - 2)
                    if e_ >= 2:
                        W("tensor", "SV", snap[("comb", b, t, e_ - 2)])
                    elif t == 1:
                        W("tensor", "SV", snap[("comb", b, 0, e_ + 6)])
                    elif b >= 1:
                        W("tensor", "SV", snap[("comb", b - 1, 1, e_ + 6)])
                    if b == 0:
                        W("tensor", "dW", snap[f"dW_e{e_}"])
                    for r in range(8):
                        for hf in range(2):
                            OP("tensor", lambda e, b=b, t=t, r=r, hf=hf, e_=e_: e.matmul(
                                ps[:, 512 * hf:512 * (hf + 1)] if False else
                                psEO[e_ % 2][:, 512 * hf:512 * (hf + 1)],
                                lhs_slice(b, t, r),
                                EW[:, (8 * e_ + r) * 1024 + 512 * hf:(8 * e_ + r) * 1024 + 512 * (hf + 1)],
                                start=(r == 0), stop=(r == 7)))
                    inc_prev("tensor", "ST")
                    snap[("eo", b, t, e_)] = now("ST")

                    if t == 0 and e_ in (0, 1, 2):
                        name, fn = next(golay_t)
                        fn()
                    if t == 0 and e_ == 3:
                        W("tensor", "SV", snap[("x1e", b, 0)])
                        W("tensor", "SV", snap[("corr", b)])
                        for hf in range(2):
                            OP("tensor", lambda e, hf=hf: e.matmul(
                                psC[:, 512 * hf:512 * (hf + 1)], CORR[:, 0:128],
                                WLE[:, 512 * hf:512 * (hf + 1)], start=True, stop=False))
                            OP("tensor", lambda e, hf=hf: e.matmul(
                                psC[:, 512 * hf:512 * (hf + 1)], ONESB[0:1, 0:128],
                                BLEB[0:1, 512 * hf:512 * (hf + 1)], start=False, stop=True))
                        inc_prev("tensor", "ST")
                        snap[("spart", b, 0)] = now("ST")
                    if t == 1 and e_ == 0:
                        W("tensor", "SV", snap[("x1e", b, 1)])
                        for hf in range(2):
                            OP("tensor", lambda e, hf=hf: e.matmul(
                                psC[:, 512 * hf:512 * (hf + 1)], CORR[:, 128:256],
                                WLE[:, 512 * hf:512 * (hf + 1)], start=True, stop=False))
                            OP("tensor", lambda e, hf=hf: e.matmul(
                                psC[:, 512 * hf:512 * (hf + 1)], ONESB[0:1, 0:128],
                                BLEB[0:1, 512 * hf:512 * (hf + 1)], start=False, stop=True))
                        inc_prev("tensor", "ST")
                        snap[("spart", b, 1)] = now("ST")

                    if t == 0 and e_ in (0, 1, 2):
                        name, fn = next(golay_v)
                        fn()
                    if t == 0 and e_ == 3 and b + 1 < NB:
                        prologue_b(b + 1)
                        prologue_c(b + 1)
                    W("vector", "ST", snap[("eo", b, t, e_)])
                    if e_ == 0:
                        W("vector", "SV", snap[("m", b)])
                        if gt >= 2:
                            W("vector", "SS", snap[("g", b - 1, t)])
                        OP("vector", lambda e, t=t, e_=e_: e.tensor_scalar(
                            ACC[t][:, :], psEO[e_ % 2][:, :], MCOL[:, 8 * t + e_:8 * t + e_ + 1],
                            None, ALU.mult), [("SV", 1)])
                    else:
                        W("vector", "SV", snap[("comb", b, t, e_ - 1)])
                        OP("vector", lambda e, t=t, e_=e_: e.scalar_tensor_tensor(
                            ACC[t][:, :], psEO[e_ % 2][:, :], MCOL[:, 8 * t + e_:8 * t + e_ + 1],
                            ACC[t][:, :], ALU.mult, ALU.add), [("SV", 1)])
                    snap[("comb", b, t, e_)] = now("SV")

                    if e_ == (4 if t == 0 else 1):
                        W("vector", "ST", snap[("spart", b, t)])
                        W("vector", "SV", snap[("x1e", b, t)])
                        OP("vector", lambda e, t=t: e.tensor_tensor(
                            S[:, N * t:N * (t + 1)], psC[:, :], X1[:, N * t:N * (t + 1)], ALU.add),
                           [("SV", 1)])
                        snap[("s", b, t)] = now("SV")

                W("scalar", "SV", snap[("comb", b, t, 7)])
                OP("scalar", lambda e, b=b, t=t: e.activation(
                    G[t][:, :], ACC[t][:, :], AF.Silu,
                    bias=CCOL[:, 2 * (b % 2) + t:2 * (b % 2) + t + 1]), [("SS", 1)])
                snap[("g", b, t)] = now("SS")

                W("vector", "SS", snap[("g", b, t)])
                W("vector", "SV", snap[("s", b, t)])
                OP("vector", lambda e, t=t: e.tensor_tensor(G[t][:, :], S[:, N * t:N * (t + 1)],
                                                            G[t][:, :], ALU.mult))
                selfsem("vector", "SV")
                OP("vector", lambda e, b=b, t=t: e.tensor_tensor(
                    XB[b % 2][:, N * t:N * (t + 1)], XB[b % 2][:, N * t:N * (t + 1)],
                    G[t][:, :], ALU.add), [("SV", 1)])
                snap[("final", b, t)] = now("SV")

                W("gpsimd", "SV", snap[("final", b, t)])
                OP("gpsimd", lambda e, b=b, t=t: e.dma_start(
                    AP(y_d, (256 * b + 128 * t) * N, [[N, 128], [1, N]]),
                    XB[b % 2][:, N * t:N * (t + 1)]), [("dOUT", 16)])

            if b + 1 < NB:
                emit_router(b + 1)

            inc_prev("tensor", "ST")
            snap[("tb_end", b)] = now("ST")

            if b + 2 < NB:
                xload(b + 2)

        W("gpsimd", "dOUT", 16 * NT)
        W("sync", "dOUT", 16 * NT)

        import kernel as _kmod
        _kmod._last_plan = {k: [(i[0], i[1], i[2]) if i[0] == "w" else ("o", None, i[2])
                                for i in v] for k, v in plan.items()}

        def run_plan(eng_obj, name):
            for item in plan[name]:
                if item[0] == "w":
                    eng_obj.wait_ge(sems[item[1]], item[2])
                else:
                    instr = item[1](eng_obj)
                    merged = {}
                    for s, k in item[2]:
                        merged[s] = merged.get(s, 0) + k
                    first = True
                    for s, k in merged.items():
                        if first:
                            instr = instr.then_inc(sems[s], k)
                            first = False
                        else:
                            eng_obj.nop().then_inc(sems[s], k)

        @block.sync
        def _(eng):
            run_plan(eng, "sync")

        @block.scalar
        def _(eng):
            run_plan(eng, "scalar")

        @block.vector
        def _(eng):
            run_plan(eng, "vector")

        @block.tensor
        def _(eng):
            run_plan(eng, "tensor")

        @block.gpsimd
        def _(eng):
            run_plan(eng, "gpsimd")

    return nc


def kernel(L0_attn_x, L0_attn_W_el, L0_attn_b_el, L0_attn_W_le, L0_attn_b_le,
           L0_ffn_router_w, L0_ffn_expert_weights):
    c = _consts()
    x = np.ascontiguousarray(np.asarray(L0_attn_x, dtype=np.float32).reshape(-1, N))
    xf = np.ascontiguousarray(x[0:1, :])
    wel = np.ascontiguousarray(np.asarray(L0_attn_W_el, dtype=np.float32))
    bel = np.ascontiguousarray(np.asarray(L0_attn_b_el, dtype=np.float32).reshape(1, 24))
    wle = np.ascontiguousarray(np.asarray(L0_attn_W_le, dtype=np.float32))
    ble = np.ascontiguousarray(np.asarray(L0_attn_b_le, dtype=np.float32).reshape(1, N))
    rw = np.ascontiguousarray(np.asarray(L0_ffn_router_w, dtype=np.float32))
    ew = np.ascontiguousarray(np.asarray(L0_ffn_expert_weights, dtype=np.float32))

    nc = build_nc()
    in_maps = []
    for core in range(8):
        shard = np.ascontiguousarray(x[core * T:(core + 1) * T])
        in_maps.append(dict(
            x=shard, xf=xf, wel=wel, bel=bel, wle=wle, ble=ble, rw=rw, ew=ew,
            tabc=c["TABC"], tabs=c["TABS"], ka=c["KA"], ka2=c["KA2"], misc=c["MISC"],
            onesf=c["ONESF"], onesb=c["ONESB"], onw=c["ONW"], idn8=c["IDN8"],
            hmat=c["HMAT"], gam=c["GAM"]))
    res = run_bass_kernel_spmd(nc, in_maps, core_ids=list(range(8)))
    y = np.concatenate([r["y"] for r in res.results], axis=0)
    return y.reshape(16, 2048, N)


# version 22
# speedup vs baseline: 1.0105x; 1.0105x over previous
import math
from collections import defaultdict

import numpy as np
import ml_dtypes

import concourse.bass as bass
import concourse.mybir as mybir
from concourse.bass_utils import run_bass_kernel_spmd

F32 = mybir.dt.float32
BF16 = mybir.dt.bfloat16
AF = mybir.ActivationFunctionType
ALU = mybir.AluOpType
AX = mybir.AxisListType

N = 1024
T = 4096
NT = 32
NB = 16
E = 8
PI = float(np.pi)

_cache = {}


def _consts():
    if "c" in _cache:
        return _cache["c"]
    n = np.arange(N, dtype=np.int64)
    ang = 2.0 * np.pi * ((n[:, None] * n[None, :]) % N).astype(np.float64) / N
    tabc = np.cos(ang).astype(ml_dtypes.bfloat16)
    tabs = np.sin(ang).astype(ml_dtypes.bfloat16)
    ka = (np.abs(np.fft.fftfreq(N)).astype(np.float32) + np.float32(1e-8))
    KA = ka.reshape(8, 128).T.copy().astype(np.float32)
    KA2 = (KA.astype(np.float64) ** 2).astype(np.float32)
    MISC = np.zeros((128, 8), np.float32)
    MISC[:, 0] = np.pi / 2
    MISC[:, 1] = 1e-5
    ONESF = np.ones((1, 128), np.float32)
    ONESB = np.ones((1, 128), ml_dtypes.bfloat16)
    ONW = np.ones((12, 1), ml_dtypes.bfloat16)
    IDN8 = np.eye(8, dtype=ml_dtypes.bfloat16)
    _A = np.array([[1,1,1,0,1,1,1,0,0,0,1,1],[1,1,0,1,1,1,0,0,0,1,1,0],[1,0,1,1,1,0,0,0,1,1,0,1],[0,1,1,1,0,0,0,1,1,0,1,1],[0,0,0,0,1,1,1,1,1,1,1,1],[1,1,1,1,0,0,0,0,0,0,0,0],[0,0,0,0,1,1,1,1,0,0,0,0],[1,1,0,0,1,1,0,0,1,1,0,0],[1,0,1,0,1,0,1,0,1,0,1,0],[0,1,0,1,0,1,0,1,0,1,0,1],[1,0,0,1,1,0,0,1,1,0,0,1],[0,1,1,0,0,1,1,0,0,1,1,0]], dtype=np.float32)
    H = np.concatenate([_A.T, np.eye(12, dtype=np.float32)], axis=1)
    HMAT = H.T.astype(ml_dtypes.bfloat16)
    lx = np.log(np.array([2., 4., 8., 16., 32.], np.float64))
    den = 5.0 * np.sum(lx * lx) - np.sum(lx) ** 2
    gam = (np.sum(lx) - 5.0 * lx) / den
    GAM = np.zeros((1, 8), np.float32)
    GAM[0, :5] = gam.astype(np.float32)
    _cache["c"] = dict(TABC=tabc, TABS=tabs, KA=KA, KA2=KA2, MISC=MISC, ONESF=ONESF,
                       ONESB=ONESB, ONW=ONW, IDN8=IDN8, HMAT=HMAT, GAM=GAM)
    return _cache["c"]


def build_nc():
    nc = bass.Bass("TRN2", target_bir_lowering=False, dynamic_dma_scratch_size=16384)

    x_d = nc.dram_tensor("x", [T, N], F32, kind="ExternalInput")
    xf_d = nc.dram_tensor("xf", [1, N], F32, kind="ExternalInput")
    wel_d = nc.dram_tensor("wel", [N, 24], F32, kind="ExternalInput")
    bel_d = nc.dram_tensor("bel", [1, 24], F32, kind="ExternalInput")
    wle_d = nc.dram_tensor("wle", [24, N], F32, kind="ExternalInput")
    ble_d = nc.dram_tensor("ble", [1, N], F32, kind="ExternalInput")
    rw_d = nc.dram_tensor("rw", [E, N], F32, kind="ExternalInput")
    ew_d = nc.dram_tensor("ew", [E, N, N], F32, kind="ExternalInput")
    tabc_d = nc.dram_tensor("tabc", [N, N], BF16, kind="ExternalInput")
    tabs_d = nc.dram_tensor("tabs", [N, N], BF16, kind="ExternalInput")
    ka_d = nc.dram_tensor("ka", [128, 8], F32, kind="ExternalInput")
    ka2_d = nc.dram_tensor("ka2", [128, 8], F32, kind="ExternalInput")
    misc_d = nc.dram_tensor("misc", [128, 8], F32, kind="ExternalInput")
    onesf_d = nc.dram_tensor("onesf", [1, 128], F32, kind="ExternalInput")
    onesb_d = nc.dram_tensor("onesb", [1, 128], BF16, kind="ExternalInput")
    onw_d = nc.dram_tensor("onw", [12, 1], BF16, kind="ExternalInput")
    idn8_d = nc.dram_tensor("idn8", [8, 8], BF16, kind="ExternalInput")
    hmat_d = nc.dram_tensor("hmat", [24, 12], BF16, kind="ExternalInput")
    gam_d = nc.dram_tensor("gam", [1, 8], F32, kind="ExternalInput")

    y_d = nc.dram_tensor("y", [T, N], F32, kind="ExternalOutput")
    xnscr = nc.dram_tensor("xnscr", [T, N], BF16)
    hhbig = nc.dram_tensor("hhbig", [1, 128 * 2177 + 2048], BF16)

    from contextlib import ExitStack
    with ExitStack() as ctx:
        sb = lambda name, shape, dt: ctx.enter_context(nc.sbuf_tensor(name, shape, dt))

        EW = sb("EWsb", [128, 64 * 1024], BF16)
        C = sb("Csb", [128, 8 * 1024], BF16)
        CW = sb("CWsb", [128, 8 * 24], BF16)
        WELB = sb("WELB", [128, 8 * 24], BF16)
        WLE = sb("WLEsb", [24, N], BF16)
        BLEB = sb("BLEB", [1, N], BF16)
        RWB = sb("RWB", [8, N], BF16)
        RWT = sb("RWT", [128, 64], BF16)
        BELC = sb("BELC", [24, 1], F32)
        MISC = sb("MISCsb", [128, 8], F32)
        ONESF = sb("ONESFsb", [1, 128], F32)
        ONESB = sb("ONESBsb", [1, 128], BF16)
        ONW = sb("ONWsb", [12, 1], BF16)
        IDN8 = sb("IDN8sb", [8, 8], BF16)
        HMAT = sb("HMATsb", [24, 12], BF16)
        XB = [sb(f"XB{i}", [128, 2 * N], F32) for i in range(2)]

        sctx = ExitStack()
        ssb = lambda name, shape, dt: sctx.enter_context(nc.sbuf_tensor(name, shape, dt))
        psT = sctx.enter_context(nc.psum_tensor("psT", [128, 64], BF16))
        TABC = ssb("TABC", [128, 8 * 1024], BF16)
        TABS = ssb("TABS", [128, 8 * 1024], BF16)
        GAM = ssb("GAMsb", [1, 8], F32)
        HH = ssb("HHsb", [1, 2 * N], BF16)
        MEANS = ssb("MEANS", [1, 8], F32)
        CNTS = ssb("CNTS", [1, 8], F32)
        LYS = ssb("LYS", [1, 8], F32)
        DV = ssb("DVsb", [1, 1], F32)
        DCOL = ssb("DCOL", [128, 1], F32)
        FCOLS = ssb("FCOLS", [128, 8], F32)
        KA = ssb("KAsb", [128, 8], F32)
        KA2 = ssb("KA2sb", [128, 8], F32)
        FW1 = ssb("FW1", [128, 8], F32)
        FW2 = ssb("FW2", [128, 8], F32)
        FW3 = ssb("FW3", [128, 8], F32)
        FW4 = ssb("FW4", [128, 8], F32)
        FRC = ssb("FRC", [128, 8], BF16)
        NFIC = ssb("NFIC", [128, 8], BF16)
        A1B = ssb("A1B", [128, 8 * 24], BF16)
        A2B = ssb("A2B", [128, 8 * 24], BF16)
        UV1 = ssb("UV1", [128, 24], F32)
        UV2 = ssb("UV2", [128, 24], F32)
        sctx.close()

        psA = ctx.enter_context(nc.psum_tensor("psA", [128, N], F32))
        psB = ctx.enter_context(nc.psum_tensor("psB", [128, N], F32))
        psS = ctx.enter_context(nc.psum_tensor("psS", [128, 512], F32))
        psLg = ctx.enter_context(nc.psum_tensor("psLg", [128, 16], F32))
        psC = ctx.enter_context(nc.psum_tensor("psC", [128, N], F32))

        XN = sb("XNsb", [128, 2 * N], BF16)
        XNT = [sb(f"XNT{i}", [128, 8 * 256], BF16) for i in range(2)]
        X1 = sb("X1sb", [128, 2 * N], BF16)
        S = sb("Ssb", [128, 2 * N], BF16)
        ACC = [sb(f"ACC{i}", [128, N], F32) for i in range(2)]
        G = [sb(f"G{i}", [128, N], BF16) for i in range(2)]
        SUMC = sb("SUMC", [128, 2], F32)
        SQC = sb("SQC", [128, 2], F32)
        MC = sb("MCsb", [128, 2], F32)
        VC = sb("VCsb", [128, 2], F32)
        T1C = sb("T1C", [128, 2], F32)
        T2C = sb("T2C", [128, 2], F32)
        LNT = sb("LNT", [128, 2], F32)
        RSTD = sb("RSTD", [128, 2], F32)
        NM2 = sb("NM2", [128, 2], F32)
        NMX = sb("NMX", [128, 2], F32)
        U = [sb(f"U{i}", [128, 8], F32) for i in range(2)]
        ZC = sb("ZC", [128, 2], F32)
        ZI = sb("ZI", [128, 2], F32)
        PR = [sb(f"PR{i}", [128, 8], F32) for i in range(2)]
        V1 = sb("V1", [128, 2], F32)
        V2 = sb("V2", [128, 2], F32)
        MK = sb("MKsb", [128, 16], F32)
        MS = sb("MSsb", [128, 16], F32)
        DN = sb("DNsb", [128, 2], F32)
        DI = sb("DIsb", [128, 2], F32)
        SL = sb("SLsb", [128, 16], F32)
        MT = sb("MTsb", [128, 16], F32)
        Q1 = sb("Q1sb", [128, 2], F32)
        Q2 = sb("Q2sb", [128, 2], F32)
        MCOL = sb("MCOL", [128, 16], F32)
        CCOL = sb("CCOL", [128, 4], F32)
        BITS = sb("BITS", [24, 256], BF16)
        SYN = sb("SYNsb", [12, 256], BF16)
        FIXT1 = sb("FIXT1", [1, 256], F32)
        FIXT2 = sb("FIXT2", [1, 256], F32)
        FIXB = sb("FIXB", [1, 256], BF16)
        FSYN = sb("FSYN", [24, 256], BF16)
        TB1 = sb("TB1", [24, 256], BF16)
        TB2 = sb("TB2", [24, 256], BF16)
        CORR = sb("CORR", [24, 256], BF16)

        sems = {}
        for s in ("dX", "dXN", "dXNT", "dOUT", "dW", "dC", "dM", "dH", "dMT",
                  "dWE0", "dWE1", "dWE2", "dWE3", "dWE4", "dWE5", "dWE6", "dWE7",
                  "SV", "SS", "ST"):
            sems[s] = ctx.enter_context(nc.semaphore(s))

        block = ctx.enter_context(nc.Block())

        plan = {k: [] for k in ("sync", "scalar", "vector", "tensor", "gpsimd")}
        ctr = defaultdict(int)
        snap = {}

        def W(eng, sem, val):
            if val > 0:
                plan[eng].append(("w", sem, val))

        def OP(eng, fn, incs=()):
            for s, k in incs:
                ctr[s] += k
            plan[eng].append(("o", fn, [list(i) for i in incs]))

        def now(s):
            return ctr[s]

        def inc_prev(eng, sem, k=1):
            for item in reversed(plan[eng]):
                if item[0] == "o":
                    ctr[sem] += k
                    item[2].append([sem, k])
                    return
            raise RuntimeError("no prior op")

        def selfsem(eng, sem):
            inc_prev(eng, sem)
            W(eng, sem, now(sem))

        def dma(eng, dst, src, sem, transpose=False):
            OP(eng, lambda e, d=dst, s=src, t=transpose: e.dma_start(d, s, transpose=t),
               [(sem, 16)])

        AP = bass.AP
        dma("sync", KA[:, :], ka_d.ap(), "dM")
        dma("sync", KA2[:, :], ka2_d.ap(), "dM")
        dma("sync", MISC[:, :], misc_d.ap(), "dM")
        dma("sync", ONESF[:, :], onesf_d.ap(), "dM")
        dma("sync", ONESB[:, :], onesb_d.ap(), "dM")
        dma("sync", ONW[:, :], onw_d.ap(), "dM")
        dma("sync", IDN8[:, :], idn8_d.ap(), "dM")
        dma("sync", HMAT[:, :], hmat_d.ap(), "dM")
        dma("sync", GAM[:, :], gam_d.ap(), "dM")
        dma("sync", XB[1][0:1, 0:N], xf_d.ap(), "dM")
        dma("sync", BELC[:, :], AP(bel_d, 0, [[1, 24], [1, 1]]), "dM")
        snap["dM_small"] = now("dM")
        for q in range(8):
            dma("sync", TABC[:, 1024 * q:1024 * (q + 1)],
                AP(tabc_d, q * 131072, [[1024, 128], [1, 1024]]), "dMT")
            dma("sync", TABS[:, 1024 * q:1024 * (q + 1)],
                AP(tabs_d, q * 131072, [[1024, 128], [1, 1024]]), "dMT")
        snap["dM_tabs"] = now("dMT")

        dma("gpsimd", WLE[:, :], wle_d.ap(), "dW")
        dma("gpsimd", BLEB[:, :], ble_d.ap(), "dW")
        dma("gpsimd", RWB[:, :], rw_d.ap(), "dW")
        for q in range(8):
            dma("gpsimd", WELB[:, 24 * q:24 * (q + 1)],
                AP(wel_d, q * 3072, [[24, 128], [1, 24]]), "dW")
        snap["dW_small"] = now("dW")
        for e in range(E):
            for h in range(2):
                dma("gpsimd", EW[:, 8192 * e + 4096 * h:8192 * e + 4096 * (h + 1)].rearrange(
                    "p (k c) -> p k c", c=1024),
                    AP(ew_d, e * N * N + h * 4 * 131072, [[1024, 128], [131072, 4], [1, 1024]]),
                    f"dWE{e}")
            snap[f"dW_e{e}"] = now(f"dWE{e}")

        def xload(b):
            if b >= 1:
                W("sync", "dX", 16 * b)
            if b >= 2:
                W("sync", "SV", snap[("final", b - 2, 1)])
                W("sync", "dOUT", 16 * (2 * (b - 2) + 2))
            dst = XB[b % 2][:, :].rearrange("p (t c) -> p t c", c=N)
            src = AP(x_d, 256 * b * N, [[N, 128], [128 * N, 2], [1, N]])
            dma("sync", dst, src, "dX")
            snap[("dX", b)] = now("dX")

        xload(0)

        W("vector", "dM", snap["dM_small"])
        for i, m in enumerate([4, 16, 64, 256, 1024]):
            OP("vector", lambda e, i=i, m=m: e.reduce_sum(MEANS[0:1, i:i + 1], XB[1][0:1, 0:m], AX.X))
        selfsem("vector", "SV")
        offs = [0, 4, 20, 84, 340]
        for i, m in enumerate([4, 16, 64, 256, 1024]):
            OP("vector", lambda e, i=i, m=m, o=offs[i]: e.tensor_scalar(
                HH[0:1, o:o + m], XB[1][0:1, 0:m], float(m), MEANS[0:1, i:i + 1], ALU.mult, ALU.is_gt))
        selfsem("vector", "SV")
        for i, m in enumerate([4, 16, 64, 256, 1024]):
            OP("vector", lambda e, i=i, m=m, o=offs[i]: e.reduce_sum(CNTS[0:1, i:i + 1], HH[0:1, o:o + m], AX.X))
        selfsem("vector", "SV")
        OP("vector", lambda e: e.tensor_scalar(CNTS[0:1, 0:5], CNTS[0:1, 0:5], 1.0, None, ALU.max),
           [("SV", 1)])
        snap["cnts"] = now("SV")
        W("sync", "SV", snap["cnts"])
        xload(1)

        W("scalar", "SV", snap["cnts"])
        OP("scalar", lambda e: e.activation(LYS[0:1, 0:5], CNTS[0:1, 0:5], AF.Ln), [("SS", 1)])
        snap["lys"] = now("SS")

        W("vector", "SS", snap["lys"])
        OP("vector", lambda e: e.tensor_tensor(LYS[0:1, 0:5], LYS[0:1, 0:5], GAM[0:1, 0:5], ALU.mult))
        selfsem("vector", "SV")
        OP("vector", lambda e: e.reduce_sum(DV[0:1, 0:1], LYS[0:1, 0:5], AX.X))
        selfsem("vector", "SV")
        OP("vector", lambda e: e.tensor_scalar(DV[0:1, 0:1], DV[0:1, 0:1], 1.0, 2.0, ALU.max, ALU.min),
           [("SV", 1)])
        snap["dval"] = now("SV")

        W("tensor", "SV", snap["dval"])
        W("tensor", "dM", snap["dM_small"])
        OP("tensor", lambda e: e.matmul(psC[:, 0:1], ONESF[0:1, :], DV[0:1, 0:1],
                                        start=True, stop=True), [("ST", 1)])
        snap["dbc"] = now("ST")

        W("vector", "ST", snap["dbc"])
        OP("vector", lambda e: e.tensor_copy(DCOL[:, :], psC[:, 0:1]))
        selfsem("vector", "SV")
        OP("vector", lambda e: e.tensor_scalar(FCOLS[:, 0:1], DCOL[:, :], 0.2, 1.0, ALU.mult, ALU.add))
        OP("vector", lambda e: e.tensor_scalar(FCOLS[:, 2:3], DCOL[:, :], 0.5, 1.0, ALU.mult, ALU.add))
        OP("vector", lambda e: e.tensor_scalar(FCOLS[:, 3:4], DCOL[:, :], 0.1, None, ALU.mult))
        selfsem("vector", "SV")
        OP("vector", lambda e: e.tensor_tensor(FCOLS[:, 1:2], FCOLS[:, 0:1], DCOL[:, :], ALU.mult))
        selfsem("vector", "SV")
        OP("vector", lambda e: e.tensor_scalar(FW1[:, :], KA[:, :], FCOLS[:, 2:3], FCOLS[:, 1:2],
                                               ALU.mult, ALU.add))
        OP("vector", lambda e: e.tensor_scalar(FW2[:, :], KA2[:, :], FCOLS[:, 3:4], None, ALU.mult))
        selfsem("vector", "SV")
        OP("vector", lambda e: e.tensor_scalar(FW3[:, :], FW1[:, :], PI, None, ALU.is_gt))
        OP("vector", lambda e: e.scalar_tensor_tensor(FW2[:, :], KA[:, :], -1.0, FW2[:, :],
                                                      ALU.mult, ALU.add))
        selfsem("vector", "SV")
        OP("vector", lambda e: e.scalar_tensor_tensor(FW1[:, :], FW3[:, :], -2.0 * PI, FW1[:, :],
                                                      ALU.mult, ALU.add))
        OP("vector", lambda e: e.tensor_scalar(FW2[:, :], FW2[:, :], FCOLS[:, 1:2], None, ALU.add),
           [("SV", 1)])
        snap["args"] = now("SV")

        W("scalar", "SV", snap["args"])
        OP("scalar", lambda e: e.activation(FW1[:, :], FW1[:, :], AF.Sin))
        OP("scalar", lambda e: e.activation(FW3[:, :], FW2[:, :], AF.Sin, bias=MISC[:, 0:1], scale=-1.0))
        selfsem("scalar", "SS")
        OP("scalar", lambda e: e.activation(FW2[:, :], FW2[:, :], AF.Sin), [("SS", 1)])
        snap["sins"] = now("SS")

        W("vector", "SS", snap["sins"])
        OP("vector", lambda e: e.tensor_tensor(FW3[:, :], FW1[:, :], FW3[:, :], ALU.mult))
        OP("vector", lambda e: e.tensor_tensor(FW2[:, :], FW1[:, :], FW2[:, :], ALU.mult))
        selfsem("vector", "SV")
        OP("vector", lambda e: e.tensor_scalar(FW3[:, :], FW3[:, :], 1.0 / N, None, ALU.mult))
        OP("vector", lambda e: e.tensor_scalar(FW2[:, :], FW2[:, :], -1.0 / N, None, ALU.mult))
        selfsem("vector", "SV")
        OP("vector", lambda e: e.tensor_copy(FRC[:, :], FW3[:, :]))
        OP("vector", lambda e: e.tensor_scalar(FW4[:, :], FW2[:, :], -1.0, None, ALU.mult))
        OP("vector", lambda e: e.tensor_copy(NFIC[:, :], FW2[:, :]), [("SV", 1)])
        snap["frc"] = now("SV")

        W("tensor", "dW", snap["dW_small"])
        for r in range(8):
            OP("tensor", lambda e, r=r: e.transpose(psT[:, 8 * r:8 * (r + 1)],
                                                    RWB[0:8, 128 * r:128 * (r + 1)],
                                                    IDN8[:, :]), [("ST", 1)] if r == 7 else ())
        W("vector", "ST", now("ST"))
        OP("vector", lambda e: e.tensor_copy(RWT[:, :], psT[:, :]), [("SV", 1)])
        W("tensor", "SV", now("SV"))
        snap["rwt"] = now("SV")

        W("tensor", "dMT", snap["dM_tabs"])
        W("tensor", "dW", snap["dW_small"])
        for r in range(8):
            for q in range(8):
                OP("tensor", lambda e, r=r, q=q: e.matmul(
                    psB[:, 24 * r:24 * (r + 1)],
                    TABC[:, 1024 * q + 128 * r:1024 * q + 128 * r + 128],
                    WELB[:, 24 * q:24 * (q + 1)], start=(q == 0), stop=(q == 7)))
            for q in range(8):
                OP("tensor", lambda e, r=r, q=q: e.matmul(
                    psB[:, 192 + 24 * r:192 + 24 * (r + 1)],
                    TABS[:, 1024 * q + 128 * r:1024 * q + 128 * r + 128],
                    WELB[:, 24 * q:24 * (q + 1)], start=(q == 0), stop=(q == 7)))
        inc_prev("tensor", "ST")
        W("vector", "ST", now("ST"))
        W("vector", "SV", snap["frc"])
        for r in range(8):
            OP("vector", lambda e, r=r: e.tensor_scalar(A1B[:, 24 * r:24 * (r + 1)],
                                                        psB[:, 24 * r:24 * (r + 1)],
                                                        FW3[:, r:r + 1], None, ALU.mult))
            OP("vector", lambda e, r=r: e.tensor_scalar(A2B[:, 24 * r:24 * (r + 1)],
                                                        psB[:, 192 + 24 * r:192 + 24 * (r + 1)],
                                                        FW3[:, r:r + 1], None, ALU.mult))
        selfsem("vector", "SV")
        for r in range(8):
            OP("vector", lambda e, r=r: e.scalar_tensor_tensor(
                A1B[:, 24 * r:24 * (r + 1)], psB[:, 192 + 24 * r:192 + 24 * (r + 1)],
                FW2[:, r:r + 1], A1B[:, 24 * r:24 * (r + 1)], ALU.mult, ALU.add))
            OP("vector", lambda e, r=r: e.scalar_tensor_tensor(
                A2B[:, 24 * r:24 * (r + 1)], psB[:, 24 * r:24 * (r + 1)],
                FW4[:, r:r + 1], A2B[:, 24 * r:24 * (r + 1)], ALU.mult, ALU.add))
        inc_prev("vector", "SV")
        W("tensor", "SV", now("SV"))
        for r in range(8):
            for q in range(8):
                OP("tensor", lambda e, r=r, q=q: e.matmul(
                    psC[:, 24 * r:24 * (r + 1)],
                    TABC[:, 1024 * q + 128 * r:1024 * q + 128 * r + 128],
                    A1B[:, 24 * q:24 * (q + 1)], start=(q == 0), stop=False))
            for q in range(8):
                OP("tensor", lambda e, r=r, q=q: e.matmul(
                    psC[:, 24 * r:24 * (r + 1)],
                    TABS[:, 1024 * q + 128 * r:1024 * q + 128 * r + 128],
                    A2B[:, 24 * q:24 * (q + 1)], start=False, stop=(q == 7)))
        inc_prev("tensor", "ST")
        W("vector", "ST", now("ST"))
        OP("vector", lambda e: e.tensor_copy(CW[:, :], psC[:, 0:192]), [("SV", 1)])
        W("tensor", "SV", snap["frc"])
        W("tensor", "dMT", snap["dM_tabs"])
        for hh_ in range(2):
            for r in range(8):
                OP("tensor", lambda e, hh_=hh_, r=r: e.matmul(
                    psA[0:1, 512 * hh_:512 * (hh_ + 1)], FRC[:, r:r + 1],
                    TABC[:, 1024 * r + 512 * hh_:1024 * r + 512 * (hh_ + 1)],
                    start=(r == 0), stop=False))
            for r in range(8):
                OP("tensor", lambda e, hh_=hh_, r=r: e.matmul(
                    psA[0:1, 512 * hh_:512 * (hh_ + 1)], NFIC[:, r:r + 1],
                    TABS[:, 1024 * r + 512 * hh_:1024 * r + 512 * (hh_ + 1)],
                    start=False, stop=(r == 7)))
        inc_prev("tensor", "ST")
        snap["hmm"] = now("ST")

        W("scalar", "ST", snap["hmm"])
        OP("scalar", lambda e: e.activation(HH[0:1, 0:N], psA[0:1, 0:N], AF.Identity))
        OP("scalar", lambda e: e.activation(HH[0:1, N:2 * N], psA[0:1, 0:N], AF.Identity),
           [("SS", 1)])
        snap["hhs"] = now("SS")

        W("sync", "SS", snap["hhs"])
        dma("sync", AP(hhbig, 0, [[1, 1], [1, 2 * N]]), HH[:, :], "dH")
        for k in range(7):
            W("sync", "dH", now("dH"))
            p = 2 ** k
            dma("sync", AP(hhbig, p * 2178, [[2177, p], [1, 2 * N]]),
                AP(hhbig, 0, [[2177, p], [1, 2 * N]]), "dH")
        W("sync", "dH", now("dH"))
        for r in range(8):
            dma("sync", C[:, 1024 * r:1024 * (r + 1)],
                AP(hhbig, N - 128 * r, [[2177, 128], [1, N]]), "dC")
        snap["dC_all"] = now("dC")

        snap["cw"] = now("SV")
        snap["cw_t"] = now("ST")
        W("vector", "ST", snap["cw_t"])
        OP("vector", lambda e: e.memset(FSYN[:, :], 0.0))

        psEO = [psA, psB]

        def lhs_slice(b, t, r):
            return XNT[b % 2][:, 256 * r + 128 * t:256 * r + 128 * t + 128]

        def prologue_a(b):
            W("scalar", "dX", snap[("dX", b)])
            if b == 0:
                W("scalar", "SV", snap["cw"])
                W("scalar", "ST", snap["cw_t"])
                W("scalar", "dH", 16)
            if b >= 1:
                W("scalar", "dXN", 16 * b)
            for t in range(2):
                OP("scalar", lambda e, b=b, t=t: e.activation(
                    XN[:, N * t:N * (t + 1)], XB[b % 2][:, N * t:N * (t + 1)], AF.Square,
                    accum_out=SQC[:, t:t + 1]))
                OP("scalar", lambda e, b=b, t=t: e.activation(
                    XN[:, N * t:N * (t + 1)], XB[b % 2][:, N * t:N * (t + 1)], AF.Identity,
                    accum_out=SUMC[:, t:t + 1]))
            inc_prev("scalar", "SS")
            snap[("sq", b)] = now("SS")

        def prologue_b(b):
            if b == 0:
                W("vector", "SV", snap["cw"])
                W("vector", "ST", snap["cw_t"])
                W("vector", "dH", 16)
            W("vector", "SS", snap[("sq", b)])
            for t in range(2):
                OP("vector", lambda e, t=t: e.tensor_scalar(MC[:, t:t + 1], SUMC[:, t:t + 1],
                                                            1.0 / N, None, ALU.mult))
                OP("vector", lambda e, t=t: e.tensor_scalar(T1C[:, t:t + 1], SQC[:, t:t + 1],
                                                            1.0 / N, None, ALU.mult))
            selfsem("vector", "SV")
            for t in range(2):
                OP("vector", lambda e, t=t: e.tensor_tensor(T2C[:, t:t + 1], MC[:, t:t + 1],
                                                            MC[:, t:t + 1], ALU.mult))
            selfsem("vector", "SV")
            for t in range(2):
                OP("vector", lambda e, t=t: e.tensor_tensor(VC[:, t:t + 1], T1C[:, t:t + 1],
                                                            T2C[:, t:t + 1], ALU.subtract))
            inc_prev("vector", "SV")
            snap[("vc", b)] = now("SV")

            W("scalar", "SV", snap[("vc", b)])
            for t in range(2):
                OP("scalar", lambda e, t=t: e.activation(LNT[:, t:t + 1], VC[:, t:t + 1], AF.Ln,
                                                         bias=MISC[:, 1:2]))
            selfsem("scalar", "SS")
            for t in range(2):
                OP("scalar", lambda e, t=t: e.activation(RSTD[:, t:t + 1], LNT[:, t:t + 1], AF.Exp,
                                                         scale=-0.5))
            inc_prev("scalar", "SS")
            snap[("rstd", b)] = now("SS")

            W("vector", "SS", snap[("rstd", b)])
            for t in range(2):
                OP("vector", lambda e, t=t: e.tensor_tensor(NM2[:, t:t + 1], RSTD[:, t:t + 1],
                                                            MC[:, t:t + 1], ALU.mult))
            selfsem("vector", "SV")
            for t in range(2):
                OP("vector", lambda e, t=t: e.tensor_scalar(NM2[:, t:t + 1], NM2[:, t:t + 1],
                                                            -1.0, None, ALU.mult))
            inc_prev("vector", "SV")
            snap[("nm", b)] = now("SV")

        def prologue_c(b):
            W("scalar", "SV", snap[("nm", b)])
            for t in range(2):
                OP("scalar", lambda e, b=b, t=t: e.activation(
                    XN[:, N * t:N * (t + 1)], XB[b % 2][:, N * t:N * (t + 1)], AF.Identity,
                    scale=RSTD[:, t:t + 1], bias=NM2[:, t:t + 1]))
            inc_prev("scalar", "SS")
            W("scalar", "SS", now("SS"))
            if b >= 1:
                W("scalar", "dXN", 16 * b)
            OP("scalar", lambda e, b=b: e.dma_start(
                AP(xnscr, 256 * b * N, [[N, 128], [128 * N, 2], [1, N]]),
                XN[:, :].rearrange("p (t c) -> p t c", c=N)), [("dXN", 16)])
            snap[("dXN", b)] = now("dXN")

            W("sync", "dXN", snap[("dXN", b)])
            if b >= 1:
                W("sync", "dXNT", 128 * b)
            if b >= 2:
                W("sync", "ST", snap[("tb_end", b - 2)])
            for r in range(8):
                dma("sync", XNT[b % 2][:, 256 * r:256 * (r + 1)],
                    AP(xnscr, (256 * b) * N + 128 * r, [[N, 256], [1, 128]]), "dXNT",
                    transpose=True)
            snap[("dXNT", b)] = now("dXNT")

        def emit_router(bb):
            if bb >= 1:
                W("tensor", "SS", snap[("exp", bb - 1, 1)])
            W("tensor", "dXNT", snap[("dXNT", bb)])
            for t in range(2):
                for r in range(8):
                    OP("tensor", lambda e, bb=bb, t=t, r=r: e.matmul(
                        psLg[:, 8 * t:8 * (t + 1)], lhs_slice(bb, t, r), RWT[:, 8 * r:8 * (r + 1)],
                        start=(r == 0), stop=(r == 7)))
            inc_prev("tensor", "ST")
            snap[("logits", bb)] = now("ST")
            W("vector", "ST", snap[("logits", bb)])
            for t in range(2):
                OP("vector", lambda e, t=t: e.reduce_max(NMX[:, t:t + 1],
                                                         psLg[:, 8 * t:8 * (t + 1)],
                                                         AX.X, negate=True))
            inc_prev("vector", "SV")
            snap[("nmx", bb)] = now("SV")
            W("scalar", "SV", snap[("nmx", bb)])
            for t in range(2):
                OP("scalar", lambda e, t=t: e.activation(U[t][:, :], psLg[:, 8 * t:8 * (t + 1)],
                                                         AF.Exp, bias=NMX[:, t:t + 1]))
            inc_prev("scalar", "SS")
            snap[("exp", bb, 1)] = now("SS")

        prologue_a(0)
        prologue_b(0)
        prologue_c(0)
        emit_router(0)

        for b in range(NB):
            W("tensor", "dXNT", snap[("dXNT", b)])
            if b >= 1:
                W("tensor", "SV", snap[("corr", b - 1)])
            for r in range(8):
                OP("tensor", lambda e, b=b, r=r: e.matmul(
                    psS[0:24, 0:256], CW[:, 24 * r:24 * (r + 1)],
                    XNT[b % 2][:, 256 * r:256 * (r + 1)], start=(r == 0), stop=(r == 7)))
            inc_prev("tensor", "ST")
            snap[("latT", b)] = now("ST")

            W("vector", "ST", snap[("latT", b)])
            OP("vector", lambda e: e.tensor_scalar(BITS[:, :], psS[0:24, 0:256], BELC[:, 0:1],
                                                   0.0, ALU.add, ALU.is_gt), [("SV", 1)])
            snap[("bits", b)] = now("SV")

            W("vector", "SS", snap[("exp", b, 1)])
            for t in range(2):
                OP("vector", lambda e, t=t: e.reduce_sum(ZC[:, t:t + 1], U[t][:, :], AX.X))
            selfsem("vector", "SV")
            for t in range(2):
                OP("vector", lambda e, t=t: e.reciprocal(ZI[:, t:t + 1], ZC[:, t:t + 1]))
            selfsem("vector", "SV")
            for t in range(2):
                OP("vector", lambda e, t=t: e.tensor_scalar(PR[t][:, :], U[t][:, :], ZI[:, t:t + 1],
                                                            None, ALU.mult))
            selfsem("vector", "SV")
            for t in range(2):
                OP("vector", lambda e, t=t: e.reduce_max(V1[:, t:t + 1], PR[t][:, :], AX.X))
            selfsem("vector", "SV")
            for t in range(2):
                OP("vector", lambda e, t=t: e.tensor_scalar(MK[:, 8 * t:8 * (t + 1)], PR[t][:, :],
                                                            V1[:, t:t + 1], None, ALU.is_ge))
            selfsem("vector", "SV")
            for t in range(2):
                OP("vector", lambda e, t=t: e.scalar_tensor_tensor(
                    MS[:, 8 * t:8 * (t + 1)], MK[:, 8 * t:8 * (t + 1)], -1e9, PR[t][:, :],
                    ALU.mult, ALU.add))
            selfsem("vector", "SV")
            for t in range(2):
                OP("vector", lambda e, t=t: e.reduce_max(V2[:, t:t + 1],
                                                         MS[:, 8 * t:8 * (t + 1)], AX.X))
            selfsem("vector", "SV")
            for t in range(2):
                OP("vector", lambda e, t=t: e.tensor_tensor(DN[:, t:t + 1], V1[:, t:t + 1],
                                                            V2[:, t:t + 1], ALU.add))
                OP("vector", lambda e, t=t: e.tensor_tensor(Q1[:, t:t + 1], V1[:, t:t + 1],
                                                            V1[:, t:t + 1], ALU.mult))
                OP("vector", lambda e, t=t: e.tensor_tensor(Q2[:, t:t + 1], V2[:, t:t + 1],
                                                            V2[:, t:t + 1], ALU.mult))
            selfsem("vector", "SV")
            for t in range(2):
                OP("vector", lambda e, t=t: e.tensor_scalar(DN[:, t:t + 1], DN[:, t:t + 1], 1e-6,
                                                            None, ALU.add))
                OP("vector", lambda e, t=t: e.tensor_tensor(Q1[:, t:t + 1], Q1[:, t:t + 1],
                                                            Q2[:, t:t + 1], ALU.add))
                OP("vector", lambda e, t=t: e.tensor_scalar(SL[:, 8 * t:8 * (t + 1)], PR[t][:, :],
                                                            V2[:, t:t + 1], None, ALU.is_ge))
            selfsem("vector", "SV")
            for t in range(2):
                OP("vector", lambda e, t=t: e.reciprocal(DI[:, t:t + 1], DN[:, t:t + 1]))
                OP("vector", lambda e, t=t: e.tensor_tensor(MT[:, 8 * t:8 * (t + 1)], PR[t][:, :],
                                                            SL[:, 8 * t:8 * (t + 1)], ALU.mult))
            selfsem("vector", "SV")
            for t in range(2):
                OP("vector", lambda e, t=t: e.tensor_scalar(MCOL[:, 8 * t:8 * (t + 1)],
                                                            MT[:, 8 * t:8 * (t + 1)],
                                                            DI[:, t:t + 1], None, ALU.mult))
                OP("vector", lambda e, b=b, t=t: e.tensor_scalar(
                    CCOL[:, 2 * (b % 2) + t:2 * (b % 2) + t + 1], Q1[:, t:t + 1], DI[:, t:t + 1],
                    None, ALU.mult))
            inc_prev("vector", "SV")
            snap[("m", b)] = now("SV")

            golay_t = iter([
                ("syn", lambda: (W("tensor", "SV", snap[("bits", b)]),
                                 OP("tensor", lambda e: e.matmul(psS[0:12, 256:512], HMAT[:, :],
                                                                 BITS[:, :], start=True, stop=True),
                                    [("ST", 1)]),
                                 snap.__setitem__(("synm", b), now("ST")))),
                ("w", lambda: (W("tensor", "SV", snap[("synv", b)]),
                               OP("tensor", lambda e: e.matmul(psS[0:1, 0:256], ONW[:, :],
                                                               SYN[:, :], start=True, stop=True),
                                  [("ST", 1)]),
                               snap.__setitem__(("wm", b), now("ST")))),
                ("fr", lambda: (W("tensor", "SV", snap[("fix", b)]),
                                OP("tensor", lambda e: e.matmul(psS[0:12, 0:256], ONESB[0:1, 0:12],
                                                                FIXB[:, :], start=True, stop=True),
                                   [("ST", 1)]),
                                snap.__setitem__(("frm", b), now("ST")))),
            ])
            golay_v = iter([
                ("synv", lambda: (W("vector", "ST", snap[("synm", b)]),
                                  OP("vector", lambda e: e.tensor_scalar(
                                      SYN[:, :], psS[0:12, 256:512], 0.5, None, ALU.is_gt),
                                     [("SV", 1)]),
                                  snap.__setitem__(("synv", b), now("SV")))),
                ("fixv", lambda: (W("vector", "ST", snap[("wm", b)]),
                                  OP("vector", lambda e: e.tensor_scalar(
                                      FIXT1[:, :], psS[0:1, 0:256], 0.5, None, ALU.is_gt)),
                                  OP("vector", lambda e: e.tensor_scalar(
                                      FIXT2[:, :], psS[0:1, 0:256], 3.5, None, ALU.is_lt)),
                                  selfsem("vector", "SV"),
                                  OP("vector", lambda e: e.tensor_tensor(
                                      FIXB[:, :], FIXT1[:, :], FIXT2[:, :], ALU.mult), [("SV", 1)]),
                                  snap.__setitem__(("fix", b), now("SV")))),
                ("corrv", lambda: (W("vector", "ST", snap[("frm", b)]),
                                   OP("vector", lambda e: e.tensor_tensor(
                                       FSYN[0:12, :], SYN[:, :], psS[0:12, 0:256], ALU.mult)),
                                   selfsem("vector", "SV"),
                                   OP("vector", lambda e: e.tensor_tensor(
                                       TB1[:, :], BITS[:, :], FSYN[:, :], ALU.mult)),
                                   OP("vector", lambda e: e.tensor_tensor(
                                       TB2[:, :], BITS[:, :], FSYN[:, :], ALU.add)),
                                   selfsem("vector", "SV"),
                                   OP("vector", lambda e: e.scalar_tensor_tensor(
                                       CORR[:, :], TB1[:, :], -2.0, TB2[:, :], ALU.mult, ALU.add),
                                      [("SV", 1)]),
                                   snap.__setitem__(("corr", b), now("SV")))),
            ])

            if b + 1 < NB:
                prologue_a(b + 1)

            for t in range(2):
                gt = 2 * b + t

                if b == 0 and t == 0:
                    W("tensor", "dC", snap["dC_all"])
                if t == 0:
                    if b >= 1:
                        W("tensor", "SV", snap[("s", b - 1, 1)])
                else:
                    W("tensor", "SV", snap[("s", b, 0)])
                for r in range(8):
                    for hf in range(2):
                        OP("tensor", lambda e, b=b, t=t, r=r, hf=hf: e.matmul(
                            psC[:, 512 * hf:512 * (hf + 1)], lhs_slice(b, t, r),
                            C[:, 1024 * r + 512 * hf:1024 * r + 512 * (hf + 1)],
                            start=(r == 0), stop=(r == 7)))
                inc_prev("tensor", "ST")
                snap[("x1", b, t)] = now("ST")

                W("vector", "ST", snap[("x1", b, t)])
                OP("vector", lambda e, t=t: e.tensor_copy(X1[:, N * t:N * (t + 1)], psC[:, :]),
                   [("SV", 1)])
                snap[("x1e", b, t)] = now("SV")

                for e_ in range(E):
                    ps = psEO[e_ % 2]
                    prev = (b, t, e_ - 2)
                    if e_ >= 2:
                        W("tensor", "SV", snap[("comb", b, t, e_ - 2)])
                    elif t == 1:
                        W("tensor", "SV", snap[("comb", b, 0, e_ + 6)])
                    elif b >= 1:
                        W("tensor", "SV", snap[("comb", b - 1, 1, e_ + 6)])
                    if b == 0:
                        W("tensor", f"dWE{e_}", snap[f"dW_e{e_}"])
                    for r in range(8):
                        for hf in range(2):
                            OP("tensor", lambda e, b=b, t=t, r=r, hf=hf, e_=e_: e.matmul(
                                ps[:, 512 * hf:512 * (hf + 1)] if False else
                                psEO[e_ % 2][:, 512 * hf:512 * (hf + 1)],
                                lhs_slice(b, t, r),
                                EW[:, (8 * e_ + r) * 1024 + 512 * hf:(8 * e_ + r) * 1024 + 512 * (hf + 1)],
                                start=(r == 0), stop=(r == 7)))
                    inc_prev("tensor", "ST")
                    snap[("eo", b, t, e_)] = now("ST")

                    if t == 0 and e_ in (0, 1, 2):
                        name, fn = next(golay_t)
                        fn()
                    if t == 0 and e_ == 3:
                        W("tensor", "SV", snap[("x1e", b, 0)])
                        W("tensor", "SV", snap[("corr", b)])
                        for hf in range(2):
                            OP("tensor", lambda e, hf=hf: e.matmul(
                                psC[:, 512 * hf:512 * (hf + 1)], CORR[:, 0:128],
                                WLE[:, 512 * hf:512 * (hf + 1)], start=True, stop=False))
                            OP("tensor", lambda e, hf=hf: e.matmul(
                                psC[:, 512 * hf:512 * (hf + 1)], ONESB[0:1, 0:128],
                                BLEB[0:1, 512 * hf:512 * (hf + 1)], start=False, stop=True))
                        inc_prev("tensor", "ST")
                        snap[("spart", b, 0)] = now("ST")
                    if t == 1 and e_ == 0:
                        W("tensor", "SV", snap[("x1e", b, 1)])
                        for hf in range(2):
                            OP("tensor", lambda e, hf=hf: e.matmul(
                                psC[:, 512 * hf:512 * (hf + 1)], CORR[:, 128:256],
                                WLE[:, 512 * hf:512 * (hf + 1)], start=True, stop=False))
                            OP("tensor", lambda e, hf=hf: e.matmul(
                                psC[:, 512 * hf:512 * (hf + 1)], ONESB[0:1, 0:128],
                                BLEB[0:1, 512 * hf:512 * (hf + 1)], start=False, stop=True))
                        inc_prev("tensor", "ST")
                        snap[("spart", b, 1)] = now("ST")

                    if t == 0 and e_ in (0, 1, 2):
                        name, fn = next(golay_v)
                        fn()
                    if t == 0 and e_ == 3 and b + 1 < NB:
                        prologue_b(b + 1)
                        prologue_c(b + 1)
                    W("vector", "ST", snap[("eo", b, t, e_)])
                    if e_ == 0:
                        W("vector", "SV", snap[("m", b)])
                        if gt >= 2:
                            W("vector", "SS", snap[("g", b - 1, t)])
                        OP("vector", lambda e, t=t, e_=e_: e.tensor_scalar(
                            ACC[t][:, :], psEO[e_ % 2][:, :], MCOL[:, 8 * t + e_:8 * t + e_ + 1],
                            None, ALU.mult), [("SV", 1)])
                    else:
                        W("vector", "SV", snap[("comb", b, t, e_ - 1)])
                        OP("vector", lambda e, t=t, e_=e_: e.scalar_tensor_tensor(
                            ACC[t][:, :], psEO[e_ % 2][:, :], MCOL[:, 8 * t + e_:8 * t + e_ + 1],
                            ACC[t][:, :], ALU.mult, ALU.add), [("SV", 1)])
                    snap[("comb", b, t, e_)] = now("SV")

                    if e_ == (4 if t == 0 else 1):
                        W("vector", "ST", snap[("spart", b, t)])
                        W("vector", "SV", snap[("x1e", b, t)])
                        OP("vector", lambda e, t=t: e.tensor_tensor(
                            S[:, N * t:N * (t + 1)], psC[:, :], X1[:, N * t:N * (t + 1)], ALU.add),
                           [("SV", 1)])
                        snap[("s", b, t)] = now("SV")

                W("scalar", "SV", snap[("comb", b, t, 7)])
                OP("scalar", lambda e, b=b, t=t: e.activation(
                    G[t][:, :], ACC[t][:, :], AF.Silu,
                    bias=CCOL[:, 2 * (b % 2) + t:2 * (b % 2) + t + 1]), [("SS", 1)])
                snap[("g", b, t)] = now("SS")

                W("vector", "SS", snap[("g", b, t)])
                W("vector", "SV", snap[("s", b, t)])
                OP("vector", lambda e, t=t: e.tensor_tensor(G[t][:, :], S[:, N * t:N * (t + 1)],
                                                            G[t][:, :], ALU.mult))
                selfsem("vector", "SV")
                OP("vector", lambda e, b=b, t=t: e.tensor_tensor(
                    XB[b % 2][:, N * t:N * (t + 1)], XB[b % 2][:, N * t:N * (t + 1)],
                    G[t][:, :], ALU.add), [("SV", 1)])
                snap[("final", b, t)] = now("SV")

                W("gpsimd", "SV", snap[("final", b, t)])
                if gt >= 1:
                    W("gpsimd", "dOUT", 16 * gt)
                OP("gpsimd", lambda e, b=b, t=t: e.dma_start(
                    AP(y_d, (256 * b + 128 * t) * N, [[N, 128], [1, N]]),
                    XB[b % 2][:, N * t:N * (t + 1)]), [("dOUT", 16)])

            if b + 1 < NB:
                emit_router(b + 1)

            inc_prev("tensor", "ST")
            snap[("tb_end", b)] = now("ST")

            if b + 2 < NB:
                xload(b + 2)

        W("gpsimd", "dOUT", 16 * NT)
        W("sync", "dOUT", 16 * NT)

        import kernel as _kmod
        _kmod._last_plan = {k: [(i[0], i[1], i[2]) if i[0] == "w" else ("o", None, i[2])
                                for i in v] for k, v in plan.items()}

        def run_plan(eng_obj, name):
            for item in plan[name]:
                if item[0] == "w":
                    eng_obj.wait_ge(sems[item[1]], item[2])
                else:
                    instr = item[1](eng_obj)
                    merged = {}
                    for s, k in item[2]:
                        merged[s] = merged.get(s, 0) + k
                    first = True
                    for s, k in merged.items():
                        if first:
                            instr = instr.then_inc(sems[s], k)
                            first = False
                        else:
                            eng_obj.nop().then_inc(sems[s], k)

        @block.sync
        def _(eng):
            run_plan(eng, "sync")

        @block.scalar
        def _(eng):
            run_plan(eng, "scalar")

        @block.vector
        def _(eng):
            run_plan(eng, "vector")

        @block.tensor
        def _(eng):
            run_plan(eng, "tensor")

        @block.gpsimd
        def _(eng):
            run_plan(eng, "gpsimd")

    return nc


def kernel(L0_attn_x, L0_attn_W_el, L0_attn_b_el, L0_attn_W_le, L0_attn_b_le,
           L0_ffn_router_w, L0_ffn_expert_weights):
    c = _consts()
    x = np.ascontiguousarray(np.asarray(L0_attn_x, dtype=np.float32).reshape(-1, N))
    xf = np.ascontiguousarray(x[0:1, :])
    wel = np.ascontiguousarray(np.asarray(L0_attn_W_el, dtype=np.float32))
    bel = np.ascontiguousarray(np.asarray(L0_attn_b_el, dtype=np.float32).reshape(1, 24))
    wle = np.ascontiguousarray(np.asarray(L0_attn_W_le, dtype=np.float32))
    ble = np.ascontiguousarray(np.asarray(L0_attn_b_le, dtype=np.float32).reshape(1, N))
    rw = np.ascontiguousarray(np.asarray(L0_ffn_router_w, dtype=np.float32))
    ew = np.ascontiguousarray(np.asarray(L0_ffn_expert_weights, dtype=np.float32))

    nc = build_nc()
    in_maps = []
    for core in range(8):
        shard = np.ascontiguousarray(x[core * T:(core + 1) * T])
        in_maps.append(dict(
            x=shard, xf=xf, wel=wel, bel=bel, wle=wle, ble=ble, rw=rw, ew=ew,
            tabc=c["TABC"], tabs=c["TABS"], ka=c["KA"], ka2=c["KA2"], misc=c["MISC"],
            onesf=c["ONESF"], onesb=c["ONESB"], onw=c["ONW"], idn8=c["IDN8"],
            hmat=c["HMAT"], gam=c["GAM"]))
    res = run_bass_kernel_spmd(nc, in_maps, core_ids=list(range(8)))
    y = np.concatenate([r["y"] for r in res.results], axis=0)
    return y.reshape(16, 2048, N)
